# version 02
# baseline (speedup 1.0000x reference)
import math
import numpy as np
import concourse.bass as bass
import concourse.mybir as mybir
from concourse.bass_utils import run_bass_kernel_spmd
from contextlib import ExitStack

F32 = mybir.dt.float32
BF16 = mybir.dt.bfloat16
I32 = mybir.dt.int32
AF = mybir.ActivationFunctionType
ALU = mybir.AluOpType
AX = mybir.AxisListType

T = 2048
D = 2048
FF = 8192
NT = 16
SQ = math.sqrt(128.0)
ENGS = ("tensor", "vector", "scalar", "gpsimd", "sync")


def slope(h):
    return 2.0 ** (-(h // 4 + 1) / 2.0)


class Prog:
    def __init__(self):
        self.items = {e: [] for e in ENGS}
        self.cnt = {}

    def now(self, sem):
        return self.cnt.get(sem, 0)

    def w(self, sem):
        return (sem, self.now(sem))

    def op(self, eng, fn, waits=(), inc=None, amt=1):
        self.items[eng].append((list(waits), fn, inc, amt))
        if inc is not None:
            self.cnt[inc] = self.cnt.get(inc, 0) + amt

    def vbub(self, waits=(), inc=None):
        self.op("vector",
                lambda e: e.tensor_copy(self.bubA[:], self.bubB[:]),
                waits, inc=inc)

    def dma(self, qeng, sem, out, in_, waits=()):
        self.op(qeng, lambda e, o=out, i=in_: e.dma_start(out=o, in_=i),
                waits, inc=sem, amt=16)

    def fence(self):
        snap = [(s, v) for s, v in self.cnt.items()]
        for e in ENGS:
            self.items[e].append((snap, None, None, 0))

    def emit(self, block, sems):
        def run(eng_obj, name):
            emitted = {}
            for waits, fn, inc, amt in self.items[name]:
                for s, v in waits:
                    if v > 0 and emitted.get(s, -1) < v:
                        eng_obj.wait_ge(sems[s], v)
                        emitted[s] = v
                if fn is not None:
                    instr = fn(eng_obj)
                    if inc is not None:
                        instr.then_inc(sems[inc], amt)

        @block.tensor
        def _(e):
            run(e, "tensor")

        @block.vector
        def _(e):
            run(e, "vector")

        @block.scalar
        def _(e):
            run(e, "scalar")

        @block.gpsimd
        def _(e):
            run(e, "gpsimd")

        @block.sync
        def _(e):
            run(e, "sync")


def build():
    nc = bass.Bass("TRN2", target_bir_lowering=False, debug=False)

    x_h = nc.dram_tensor("x", [T, D], F32, kind="ExternalInput").ap()
    wq_h = nc.dram_tensor("wq", [D, D], F32, kind="ExternalInput").ap()
    wk_h = nc.dram_tensor("wk", [D, 512], F32, kind="ExternalInput").ap()
    wv_h = nc.dram_tensor("wv", [D, 512], F32, kind="ExternalInput").ap()
    wo_h = nc.dram_tensor("wo", [D, D], F32, kind="ExternalInput").ap()
    wup_h = nc.dram_tensor("wup", [FF, D], F32, kind="ExternalInput").ap()
    wdn_h = nc.dram_tensor("wdn", [D, FF], F32, kind="ExternalInput").ap()
    nw_h = nc.dram_tensor("nw", [1, D], F32, kind="ExternalInput").ap()
    gam_h = nc.dram_tensor("gam", [1, D], F32, kind="ExternalInput").ap()
    out_h = nc.dram_tensor("out", [T, D], F32, kind="ExternalOutput").ap()

    xT_h = nc.dram_tensor("xTs", [D, T], F32, kind="Internal").ap()
    yT_h = nc.dram_tensor("yTs", [D, T], F32, kind="Internal").ap()
    x1_h = nc.dram_tensor("x1s", [T, D], F32, kind="Internal").ap()
    wupT_h = nc.dram_tensor("wupT", [D, FF], BF16, kind="Internal").ap()
    wdnT_h = nc.dram_tensor("wdnT", [FF, D], BF16, kind="Internal").ap()
    vecs_h = nc.dram_tensor("vecs", [1, T], F32, kind="Internal").ap()
    scal_h = nc.dram_tensor("scal", [1, 2], F32, kind="Internal").ap()

    xT3 = xT_h.rearrange("(c p) t -> p c t", p=128)
    wupT3 = wupT_h.rearrange("(c p) h -> p c h", p=128)
    wdnT3 = wdnT_h.rearrange("(b p) c -> p b c", p=128)
    vecs2 = vecs_h.rearrange("a (b c) -> (a b) c", b=16)

    p = Prog()
    SEMS = ("qld", "gld", "tdn", "vdn", "adn", "gdn")

    ctx = ExitStack()
    with ctx:
        sb = lambda name, shape, dt: ctx.enter_context(nc.sbuf_tensor(name, shape, dt))
        sems = {s: ctx.enter_context(nc.semaphore(s)) for s in SEMS}

        ident = sb("ident", [128, 128], F32)
        p.bubA = sb("bubA", [128, 1], F32)
        p.bubB = sb("bubB", [128, 1], F32)
        identb = sb("identb", [128, 128], BF16)
        ones = sb("ones", [128, 1], F32)
        epsln = sb("epsln", [128, 1], F32)
        epsrms = sb("epsrms", [128, 1], F32)
        STNAMES = ["m0", "m1", "bias0", "bias1", "sig0", "sig1", "r0", "r1",
                   "sp0", "sp1", "sp2", "sp3", "sums", "nm", "ssq", "t1",
                   "m2t", "var", "sd", "rstd", "gmax", "gclp", "rg", "c",
                   "sx", "tstat", "waccu", "waccd"]
        stt = {n: sb("st_" + n, [128, 1], F32) for n in STNAMES}
        mean_bc = [sb(f"mean_bc{i}", [128, 1], F32) for i in range(2)]
        sval_bc = [sb(f"sval_bc{i}", [128, 1], F32) for i in range(2)]
        thrp_bc = [sb(f"thrp_bc{i}", [128, 1], F32) for i in range(2)]
        thrn_bc = [sb(f"thrn_bc{i}", [128, 1], F32) for i in range(2)]
        mean_sb = sb("mean_sb", [1, 2], F32)
        sxs_stash = sb("sxs_stash", [128, 16], F32)
        sxrow = sb("sxrow", [16, 128], F32)
        banks = [ctx.enter_context(nc.psum_tensor(f"pb{i}", [128, 512], F32))
                 for i in range(6)]

        PARN = ("m", "bias", "sig", "r")

        def sc(name, par=0):
            key = f"{name}{par}" if name in PARN else name
            return stt[key][:]

        def gop(fn, waits=()):
            p.op("gpsimd", fn, waits, inc="gdn")

        gop(lambda e: e.memset(ones[:], 1.0))
        gop(lambda e: e.memset(p.bubB[:], 0.0))
        gop(lambda e: e.memset(epsln[:], 1e-5))
        gop(lambda e: e.memset(epsrms[:], 1e-6))
        for t_, d_ in ((ident, F32), (identb, BF16)):
            gop(lambda e, t2=t_: e.memset(t2[:], 1.0))
            gop(lambda e, t2=t_: e.affine_select(
                t2[:], t2[:], [[-1, 128]], ALU.is_ge, 0.0,
                base=0, channel_multiplier=1))
            gop(lambda e, t2=t_: e.affine_select(
                t2[:], t2[:], [[1, 128]], ALU.is_ge, 0.0,
                base=0, channel_multiplier=-1))

        bf_ctx = ExitStack()
        with bf_ctx:
            wstage = [bf_ctx.enter_context(
                nc.sbuf_tensor(f"wst{i}", [128, 2048], F32)) for i in range(4)]

            actx = ExitStack()
            with actx:
                asb = lambda name, shape, dt: actx.enter_context(
                    nc.sbuf_tensor(name, shape, dt))
                Ccol = asb("Ccol", [128, T], F32)
                maskv = asb("maskv", [128, 4 * 512], F32)
                K_all = asb("K_all", [128, 4 * T], F32)
                V_all = asb("V_all", [128, 16 * 512], F32)
                Qpair = asb("Qpair", [128, 2 * T], F32)
                Ssb = [asb(f"Ssb{i}", [128, T], F32) for i in range(2)]
                Psb = asb("Psb", [128, T], F32)
                dtmp = asb("dtmp", [128, 512], F32)
                PTb = asb("PTb", [128, 16 * 256], F32)
                yh = [asb(f"yh{i}", [128, T], F32) for i in range(2)]
                wkvst = [asb(f"wkv{i}", [128, 512], F32) for i in range(2)]
                wqst = [asb(f"wqst{i}", [128, 256], F32) for i in range(2)]
                ptp = actx.enter_context(
                    nc.psum_tensor("ptp", [128, 1024], F32))
                ptp3 = ptp[:].rearrange("p (a b) -> p a b", a=2)
                PTb3 = PTb[:].rearrange("p (k ab) -> p k ab", ab=256)

                gop(lambda e: e.memset(K_all[:], 0.0))
                gop(lambda e: e.memset(V_all[:], 0.0))
                gop(lambda e: e.iota(Ccol[:], [[1, T]], base=0,
                                     channel_multiplier=0,
                                     allow_small_or_imprecise_dtypes=True))
                for v in range(4):
                    gop(lambda e, v=v: e.memset(
                        maskv[:, v * 512:(v + 1) * 512], 0.0))
                    gop(lambda e, v=v: e.affine_select(
                        maskv[:, v * 512:(v + 1) * 512],
                        maskv[:, v * 512:(v + 1) * 512],
                        [[-1, 512]], ALU.is_ge, -1e33,
                        base=128 * v, channel_multiplier=1))
                A_DONE = p.w("gdn")

                for ti in range(NT):
                    slot = wstage[ti % 2]
                    p.dma("gpsimd", "gld", slot[:],
                          x_h[ti * 128:(ti + 1) * 128, :],
                          waits=[p.w("tdn"), A_DONE])
                    ld = p.w("gld")
                    half = PTb[:, (ti % 2) * 2048:(ti % 2) * 2048 + 2048]
                    for dc in range(16):
                        bk = banks[dc % 2]
                        p.op("tensor",
                             lambda e, s=slot, d=dc, b=bk: e.transpose(
                                 b[:, 0:128], s[:, d * 128:(d + 1) * 128],
                                 ident[:]),
                             waits=[ld, p.w("vdn"), A_DONE], inc="tdn")
                        p.op("vector",
                             lambda e, b=bk, hf=half, d=dc: e.tensor_copy(
                                 hf[:, d * 128:(d + 1) * 128], b[:, 0:128]),
                             waits=[p.w("tdn"), p.w("gld")], inc="vdn")
                    p.dma("gpsimd", "gld", xT3[:, :, ti * 128:(ti + 1) * 128],
                          half.rearrange("p (c t) -> p c t", c=16),
                          waits=[p.w("vdn")])
                p.fence()

                for pi in range(4):
                    kv = pi
                    for dc in range(16):
                        xslot = wstage[dc % 3]
                        p.dma("sync", "qld", xslot[:],
                              xT_h[dc * 128:(dc + 1) * 128, :],
                              waits=[p.w("tdn")])
                        wslot = wkvst[dc % 2]
                        p.dma("sync", "qld", wslot[:],
                              wv_h[dc * 128:(dc + 1) * 128, :],
                              waits=[p.w("tdn")])
                        qslot = wqst[dc % 2]
                        p.dma("sync", "qld", qslot[:, 0:128],
                              wk_h[dc * 128:(dc + 1) * 128,
                                   kv * 128:(kv + 1) * 128],
                              waits=[p.w("tdn")])
                        ld = p.w("qld")
                        first, last = dc == 0, dc == 15
                        pe = p.w("vdn")
                        for kt in range(4):
                            p.op("tensor",
                                 lambda e, q=qslot, xs=xslot, k=kt, f=first,
                                 l=last: e.matmul(
                                     banks[k][:], q[:, 0:128],
                                     xs[:, k * 512:(k + 1) * 512],
                                     start=f, stop=l),
                                 waits=[ld, pe])
                        for vt in range(4):
                            ti = pi * 4 + vt
                            dst = (banks[4][:] if vt == 0 else banks[5][:]
                                   if vt == 1 else
                                   ptp[:, (vt - 2) * 512:(vt - 1) * 512])
                            p.op("tensor",
                                 lambda e, xs=xslot, w2=wslot, t=ti, d=dst,
                                 f=first, l=last: e.matmul(
                                     d, xs[:, t * 128:(t + 1) * 128],
                                     w2[:], start=f, stop=l),
                                 waits=[ld, pe],
                                 inc="tdn" if vt == 3 else None)
                    mmd = p.w("tdn")
                    for kt in range(4):
                        p.op("vector",
                             lambda e, k=kt, kvv=kv: e.tensor_copy(
                                 K_all[:, kvv * T + k * 512:
                                       kvv * T + (k + 1) * 512],
                                 banks[k][:]),
                             waits=[mmd], inc="vdn")
                    for vt in range(4):
                        ti = pi * 4 + vt
                        src = (banks[4][:] if vt == 0 else banks[5][:]
                               if vt == 1 else
                               ptp[:, (vt - 2) * 512:(vt - 1) * 512])
                        p.op("vector",
                             lambda e, s=src, t=ti: e.tensor_copy(
                                 V_all[:, t * 512:(t + 1) * 512], s),
                             waits=[mmd], inc="vdn")
                p.fence()

                for pr in range(8):
                    for dc in range(16):
                        xslot = wstage[dc % 3]
                        p.dma("sync", "qld", xslot[:],
                              xT_h[dc * 128:(dc + 1) * 128, :],
                              waits=[p.w("tdn")])
                        qslot = wqst[dc % 2]
                        p.dma("sync", "qld", qslot[:],
                              wq_h[dc * 128:(dc + 1) * 128,
                                   pr * 256:(pr + 1) * 256],
                              waits=[p.w("tdn")])
                        ld = p.w("qld")
                        first, last = dc == 0, dc == 15
                        pe = p.w("vdn")
                        for gi in range(8):
                            hh, tg = gi // 4, gi % 4
                            dst = (banks[gi][:] if gi < 6
                                   else ptp[:, (gi - 6) * 512:(gi - 5) * 512])
                            p.op("tensor",
                                 lambda e, q=qslot, xs=xslot, h2=hh, t=tg,
                                 d=dst, f=first, l=last: e.matmul(
                                     d, q[:, h2 * 128:(h2 + 1) * 128],
                                     xs[:, t * 512:(t + 1) * 512],
                                     start=f, stop=l),
                                 waits=[ld, pe],
                                 inc="tdn" if gi == 7 else None)
                    qmm = p.w("tdn")
                    for gi in range(8):
                        hh, tg = gi // 4, gi % 4
                        scl = 1.0 / (SQ * slope(2 * pr + hh))
                        src = (banks[gi][:] if gi < 6
                               else ptp[:, (gi - 6) * 512:(gi - 5) * 512])
                        p.op("vector",
                             lambda e, s=src, h2=hh, t=tg, sc2=scl:
                             e.tensor_scalar_mul(
                                 Qpair[:, h2 * T + t * 512:
                                       h2 * T + (t + 1) * 512], s, sc2),
                             waits=[qmm], inc="vdn")
                    QEV = p.w("vdn")

                    for hh in range(2):
                        h = 2 * pr + hh
                        kv = h // 4
                        s_h = slope(h)

                        def emit_scores(qi):
                            par = qi % 2
                            nkt = qi // 4 + 1
                            kpad = 512 * nkt
                            dkt = qi // 4
                            mvar = qi % 4
                            smm = []
                            for kt in range(nkt):
                                p.op("tensor",
                                     lambda e, k=kt, q=qi, h2=hh, kvv=kv:
                                     e.matmul(
                                         banks[k][:],
                                         Qpair[:, h2 * T + q * 128:
                                               h2 * T + (q + 1) * 128],
                                         K_all[:, kvv * T + k * 512:
                                               kvv * T + (k + 1) * 512],
                                         start=True, stop=True),
                                     waits=[QEV, p.w("vdn")], inc="tdn")
                                smm.append(p.w("tdn"))
                            for kt in range(nkt):
                                if kt == dkt:
                                    p.op("vector",
                                         lambda e, k=kt: e.tensor_add(
                                             dtmp[:], banks[k][:],
                                             Ccol[:, k * 512:(k + 1) * 512]),
                                         waits=[smm[kt]], inc="vdn")
                                    p.op("vector",
                                         lambda e, k=kt, pa=par, v=mvar:
                                         e.tensor_add(
                                             Ssb[pa][:, k * 512:(k + 1) * 512],
                                             dtmp[:],
                                             maskv[:, v * 512:(v + 1) * 512]),
                                         waits=[], inc="vdn")
                                else:
                                    p.op("vector",
                                         lambda e, k=kt, pa=par: e.tensor_add(
                                             Ssb[pa][:, k * 512:(k + 1) * 512],
                                             banks[k][:],
                                             Ccol[:, k * 512:(k + 1) * 512]),
                                         waits=[smm[kt]], inc="vdn")
                            p.op("vector",
                                 lambda e, pa=par, kp=kpad: e.reduce_max(
                                     sc("m", pa), Ssb[pa][:, 0:kp], axis=AX.X),
                                 waits=[], inc="vdn")
                            p.vbub()
                            p.op("vector",
                                 lambda e, pa=par, s2=s_h: e.tensor_scalar_mul(
                                     sc("bias", pa), sc("m", pa), -s2),
                                 waits=[], inc="vdn")
                            p.vbub()
                            p.vbub(inc="vdn")
                            vd = p.w("vdn")
                            p.op("scalar",
                                 lambda e, pa=par, kp=kpad, s2=s_h:
                                 e.activation(
                                     Psb[:, 0:kp], Ssb[pa][:, 0:kp], AF.Exp,
                                     bias=sc("bias", pa), scale=s2,
                                     accum_out=sc("sig", pa)),
                                 waits=[vd], inc="adn")
                            ad = p.w("adn")
                            p.vbub(waits=[ad])
                            p.vbub()
                            p.op("vector",
                                 lambda e, pa=par: e.reciprocal(
                                     sc("r", pa), sc("sig", pa)),
                                 waits=[], inc="vdn")
                            p.vbub()
                            p.op("vector",
                                 lambda e, pa=par, kp=kpad: e.tensor_scalar(
                                     Ssb[pa][:, 0:kp], Psb[:, 0:kp],
                                     sc("r", pa), None, ALU.mult),
                                 waits=[], inc="vdn")

                        def emit_pt_pv(qi):
                            par = qi % 2
                            nkt = qi // 4 + 1
                            kpad = 512 * nkt
                            nkb = kpad // 128
                            pnd = p.w("vdn")
                            for kb0 in range(0, nkb, 2):
                                n2 = min(2, nkb - kb0)
                                for j in range(n2):
                                    p.op("tensor",
                                         lambda e, pa=par, k=kb0 + j, hl=j:
                                         e.transpose(
                                             ptp[:, hl * 512:hl * 512 + 128],
                                             Ssb[pa][:, k * 128:(k + 1) * 128],
                                             ident[:]),
                                         waits=[pnd, p.w("vdn")], inc="tdn")
                                td = p.w("tdn")
                                off = (qi % 2) * 128
                                p.op("vector",
                                     lambda e, k0=kb0, n=n2, o=off:
                                     e.tensor_copy(
                                         PTb3[:, k0:k0 + n, o:o + 128],
                                         ptp3[:, 0:n, 0:128]),
                                     waits=[td], inc="vdn")
                            if qi % 2 == 1:
                                qg = qi // 2
                                nkc = kpad // 128
                                ptev = p.w("vdn")
                                for kc in range(nkc):
                                    p.op("tensor",
                                         lambda e, k=kc, kvv=kv,
                                         f=(kc == 0), l=(kc == nkc - 1):
                                         e.matmul(
                                             banks[4][:, 0:256],
                                             V_all[:, k * 512 + kvv * 128:
                                                   k * 512 + (kvv + 1) * 128],
                                             PTb[:, k * 256:(k + 1) * 256],
                                             start=f, stop=l),
                                         waits=[ptev, p.w("vdn")],
                                         inc="tdn" if kc == nkc - 1 else None)
                                pvd = p.w("tdn")
                                p.op("vector",
                                     lambda e, g=qg, h2=hh: e.tensor_copy(
                                         yh[h2][:, g * 256:(g + 1) * 256],
                                         banks[4][:, 0:256]),
                                     waits=[pvd, p.w("gld")], inc="vdn")

                        for qi in range(NT):
                            emit_scores(qi)
                            if qi >= 1:
                                emit_pt_pv(qi - 1)
                        emit_pt_pv(NT - 1)
                        p.dma("gpsimd", "gld", yT_h[h * 128:(h + 1) * 128, :],
                              yh[hh][:], waits=[p.w("vdn")])
                p.fence()

            ectx = ExitStack()
            with ectx:
                esb = lambda name, shape, dt: ectx.enter_context(
                    nc.sbuf_tensor(name, shape, dt))
                tern1 = esb("tern1", [128, 2048], F32)
                tern2 = esb("tern2", [128, 2048], F32)
                ternb = esb("ternb", [128, 2048], BF16)
                tstage = esb("tstage", [128, 8192], BF16)
                bbk = [ectx.enter_context(
                    nc.psum_tensor(f"bb{i}", [128, 1024], BF16))
                    for i in range(2)]

                for wi in range(2):
                    acc = sc("waccu") if wi == 0 else sc("waccd")
                    for ch in range(64):
                        slot = wstage[ch % 4]
                        if wi == 0:
                            p.dma("sync", "qld", slot[:],
                                  wup_h[ch * 128:(ch + 1) * 128, :],
                                  waits=[p.w("vdn")])
                        else:
                            cc, qq = ch // 4, ch % 4
                            p.dma("sync", "qld", slot[:],
                                  wdn_h[cc * 128:(cc + 1) * 128,
                                        qq * 2048:(qq + 1) * 2048],
                                  waits=[p.w("vdn")])
                        ld = p.w("qld")
                        p.op("vector",
                             lambda e, s=slot: e.reduce_sum(
                                 sc("tstat"), s[:], axis=AX.X,
                                 apply_absolute_value=True),
                             waits=[ld], inc="vdn")
                        p.vbub()
                        if ch == 0:
                            p.op("vector",
                                 lambda e, a=acc: e.tensor_copy(a, sc("tstat")),
                                 waits=[], inc="vdn")
                        else:
                            p.op("vector",
                                 lambda e, a=acc: e.scalar_tensor_tensor(
                                     a, sc("tstat"), 1.0, a, ALU.mult, ALU.add),
                                 waits=[], inc="vdn")
                    p.vbub()
                    p.vbub(inc="vdn")
                    vd = p.w("vdn")
                    p.op("tensor",
                         lambda e, a=acc: e.matmul(banks[0][0:1, 0:1], a,
                                                   ones[:], start=True,
                                                   stop=True),
                         waits=[vd], inc="tdn")
                    td = p.w("tdn")
                    p.op("vector",
                         lambda e, w2=wi: e.tensor_scalar_mul(
                             mean_sb[0:1, w2:w2 + 1], banks[0][0:1, 0:1],
                             1.0 / (FF * D)),
                         waits=[td], inc="vdn")
                p.dma("gpsimd", "gld", scal_h[:], mean_sb[:],
                      waits=[p.w("vdn")])
                for w2 in range(2):
                    p.dma("sync", "qld", mean_bc[w2][:],
                          scal_h[:, w2:w2 + 1].partition_broadcast(128),
                          waits=[p.w("gld")])
                mbcl = p.w("qld")
                for w2 in range(2):
                    p.op("vector",
                         lambda e, i=w2: e.tensor_scalar_max(
                             sval_bc[i][:], mean_bc[i][:], 1e-8),
                         waits=[mbcl], inc="vdn")
                    p.vbub()
                    p.op("vector",
                         lambda e, i=w2: e.tensor_scalar_mul(
                             thrp_bc[i][:], sval_bc[i][:], 0.5),
                         waits=[], inc="vdn")
                    p.vbub()
                    p.op("vector",
                         lambda e, i=w2: e.tensor_scalar_mul(
                             thrn_bc[i][:], thrp_bc[i][:], -1.0),
                         waits=[], inc="vdn")
                p.vbub()
                p.vbub(inc="vdn")
                THRLD = p.w("vdn")

                def tern_chunk(slot, wi2, nblk, blk_dst, ldq):
                    p.op("vector",
                         lambda e, s=slot, i=wi2: e.tensor_scalar(
                             tern1[:], s[:], thrp_bc[i][:], None, ALU.is_ge),
                         waits=[ldq, THRLD, p.w("tdn")],
                         inc="vdn")
                    p.op("vector",
                         lambda e, s=slot, i=wi2: e.tensor_scalar(
                             tern2[:], s[:], thrn_bc[i][:], None, ALU.is_le),
                         waits=[], inc="vdn")
                    p.op("vector",
                         lambda e: e.tensor_sub(ternb[:], tern1[:], tern2[:]),
                         waits=[], inc="vdn")
                    t0 = p.w("vdn")
                    for b in range(nblk):
                        bi = b % 2
                        p.op("tensor",
                             lambda e, b2=b, i=bi: e.transpose(
                                 bbk[i][:, 0:128],
                                 ternb[:, b2 * 128:(b2 + 1) * 128],
                                 identb[:]),
                             waits=[t0, p.w("vdn")], inc="tdn")
                        p.op("vector",
                             lambda e, b2=b, i=bi, d=blk_dst: e.tensor_copy(
                                 tstage[:, (d + b2) * 128:(d + b2 + 1) * 128],
                                 bbk[i][:, 0:128]),
                             waits=[p.w("tdn")] + ([p.w("gld")] if b == 0 else []),
                             inc="vdn")

                for ch in range(64):
                    slot = wstage[ch % 4]
                    p.dma("sync", "qld", slot[:],
                          wup_h[ch * 128:(ch + 1) * 128, :], waits=[p.w("vdn")])
                    ldq = p.w("qld")
                    tern_chunk(slot, 0, 16, 0, ldq)
                    p.dma("gpsimd", "gld",
                          wupT3[:, :, ch * 128:(ch + 1) * 128],
                          tstage[:, 0:2048].rearrange("p (c t) -> p c t", c=16),
                          waits=[p.w("vdn")])
                for cc in range(16):
                    for qq in range(4):
                        slot = wstage[qq % 4]
                        p.dma("sync", "qld", slot[:],
                              wdn_h[cc * 128:(cc + 1) * 128,
                                    qq * 2048:(qq + 1) * 2048],
                              waits=[p.w("vdn")])
                        ldq = p.w("qld")
                        tern_chunk(slot, 1, 16, qq * 16, ldq)
                    p.dma("gpsimd", "gld",
                          wdnT3[:, :, cc * 128:(cc + 1) * 128],
                          tstage[:].rearrange("p (b c) -> p b c", b=64),
                          waits=[p.w("vdn")])
                p.fence()

            xqT = ctx.enter_context(
                nc.sbuf_tensor("xqT", [128, 16 * 2048], BF16, side="right"))

            if True:
                fctx = ExitStack()
                with fctx:
                    fsb = lambda name, shape, dt: fctx.enter_context(
                        nc.sbuf_tensor(name, shape, dt))
                    xres = [fsb(f"xres{i}", [128, 2048], F32) for i in range(2)]
                    f1 = fsb("f1", [128, 2048], F32)
                    f2 = fsb("f2", [128, 2048], F32)
                    f3 = fsb("f3", [128, 2048], F32)
                    f4 = fsb("f4", [128, 2048], F32)
                    f5 = fsb("f5", [128, 2048], F32)
                    i32t = fsb("i32t", [128, 2048], I32)
                    nw_rep = fsb("nw_rep", [128, 2048], F32)
                    fps = fctx.enter_context(
                        nc.psum_tensor("fps", [128, 1024], F32))

                    p.dma("sync", "qld", nw_rep[:],
                          nw_h.partition_broadcast(128), waits=[])
                    NWLD = p.w("qld")

                    for tp in range(8):
                        for t2 in range(2):
                            ti = tp * 2 + t2
                            p.dma("gpsimd", "gld", xres[t2][:],
                                  x_h[ti * 128:(ti + 1) * 128, :],
                                  waits=[p.w("vdn")])
                        XR = p.w("gld")
                        for fc in range(16):
                            yst = wstage[2 * (fc % 2)]
                            wst = wstage[2 * (fc % 2) + 1]
                            p.dma("sync", "qld", yst[:],
                                  yT_h[fc * 128:(fc + 1) * 128, :],
                                  waits=[p.w("tdn")])
                            p.dma("sync", "qld", wst[:],
                                  wo_h[fc * 128:(fc + 1) * 128, :],
                                  waits=[p.w("tdn")])
                            ld = p.w("qld")
                            first, last = fc == 0, fc == 15
                            pe = p.w("vdn")
                            for gi in range(8):
                                t2, ct = gi // 4, gi % 4
                                dst = (banks[gi][:] if gi < 6
                                       else fps[:, (gi - 6) * 512:(gi - 5) * 512])
                                p.op("tensor",
                                     lambda e, y=yst, w2=wst, tt=t2, c=ct,
                                     d=dst, f=first, l=last, t0=tp:
                                     e.matmul(
                                         d,
                                         y[:, (t0 * 2 + tt) * 128:
                                           (t0 * 2 + tt + 1) * 128],
                                         w2[:, c * 512:(c + 1) * 512],
                                         start=f, stop=l),
                                     waits=[ld, pe],
                                     inc="tdn" if gi == 7 else None)
                        OMM = p.w("tdn")
                        for t2 in range(2):
                            ti = tp * 2 + t2
                            for ct in range(4):
                                gi = t2 * 4 + ct
                                src = (banks[gi][:] if gi < 6
                                       else fps[:, (gi - 6) * 512:(gi - 5) * 512])
                                p.op("vector",
                                     lambda e, s=src, c=ct, xr=xres[t2], cn=ct:
                                     e.scalar_tensor_tensor(
                                         f1[:, c * 512:(c + 1) * 512], s, 1.0,
                                         xr[:, c * 512:(c + 1) * 512],
                                         ALU.mult, ALU.add,
                                         accum_out=sc("sp%d" % cn)),
                                     waits=[OMM, XR], inc="vdn")
                            p.vbub()
                            p.op("vector", lambda e: e.tensor_add(
                                sc("sums"), sc("sp0"), sc("sp1")),
                                waits=[], inc="vdn")
                            p.op("vector", lambda e: e.tensor_add(
                                sc("t1"), sc("sp2"), sc("sp3")),
                                waits=[], inc="vdn")
                            p.vbub()
                            p.op("vector", lambda e: e.tensor_add(
                                sc("sums"), sc("sums"), sc("t1")),
                                waits=[], inc="vdn")
                            p.vbub()
                            p.op("vector", lambda e: e.tensor_scalar_mul(
                                sc("nm"), sc("sums"), -1.0 / D),
                                waits=[], inc="vdn")
                            p.op("vector", lambda e: e.scalar_tensor_tensor(
                                f4[:], f1[:], 1.0, f1[:], ALU.mult, ALU.mult,
                                accum_out=sc("ssq")),
                                waits=[], inc="vdn")
                            p.vbub()
                            p.op("vector", lambda e: e.tensor_scalar_mul(
                                sc("t1"), sc("ssq"), 1.0 / D),
                                waits=[], inc="vdn")
                            p.op("vector", lambda e: e.tensor_mul(
                                sc("m2t"), sc("nm"), sc("nm")),
                                waits=[], inc="vdn")
                            p.vbub()
                            p.op("vector", lambda e: e.tensor_sub(
                                sc("var"), sc("t1"), sc("m2t")),
                                waits=[], inc="vdn")
                            p.vbub()
                            p.vbub(inc="vdn")
                            vd = p.w("vdn")
                            p.op("scalar", lambda e: e.activation(
                                sc("sd"), sc("var"), AF.Sqrt, bias=epsln[:]),
                                waits=[vd], inc="adn")
                            ad = p.w("adn")
                            p.vbub(waits=[ad])
                            p.vbub()
                            p.op("vector", lambda e: e.reciprocal(
                                sc("rstd"), sc("sd")), waits=[], inc="vdn")
                            p.op("vector", lambda e: e.tensor_scalar(
                                f2[:], f1[:], sc("nm"), None, ALU.add),
                                waits=[], inc="vdn")
                            p.op("vector", lambda e: e.tensor_scalar(
                                f3[:], f2[:], sc("rstd"), None, ALU.mult),
                                waits=[p.w("gld")], inc="vdn")
                            p.dma("gpsimd", "gld",
                                  x1_h[ti * 128:(ti + 1) * 128, :], f3[:],
                                  waits=[p.w("vdn")])
                            p.op("vector", lambda e: e.scalar_tensor_tensor(
                                f4[:], f3[:], 1.0, f3[:], ALU.mult, ALU.mult,
                                accum_out=sc("ssq")),
                                waits=[], inc="vdn")
                            p.vbub()
                            p.op("vector", lambda e: e.tensor_scalar_mul(
                                sc("t1"), sc("ssq"), 1.0 / D),
                                waits=[], inc="vdn")
                            p.vbub()
                            p.vbub(inc="vdn")
                            vd = p.w("vdn")
                            p.op("scalar", lambda e: e.activation(
                                sc("sd"), sc("t1"), AF.Sqrt, bias=epsrms[:]),
                                waits=[vd], inc="adn")
                            ad = p.w("adn")
                            p.vbub(waits=[ad])
                            p.vbub()
                            p.op("vector", lambda e: e.reciprocal(
                                sc("rstd"), sc("sd")), waits=[], inc="vdn")
                            p.vbub()
                            p.op("vector", lambda e: e.tensor_scalar(
                                f2[:], f3[:], sc("rstd"), None, ALU.mult),
                                waits=[], inc="vdn")
                            p.op("vector", lambda e: e.tensor_mul(
                                f4[:], f2[:], nw_rep[:]),
                                waits=[NWLD], inc="vdn")
                            p.op("vector", lambda e: e.reduce_max(
                                sc("gmax"), f4[:], axis=AX.X,
                                apply_absolute_value=True),
                                waits=[], inc="vdn")
                            p.vbub()
                            p.op("vector", lambda e: e.tensor_scalar_max(
                                sc("gclp"), sc("gmax"), 1e-8),
                                waits=[], inc="vdn")
                            p.vbub()
                            p.op("vector", lambda e: e.reciprocal(
                                sc("rg"), sc("gclp")), waits=[], inc="vdn")
                            p.vbub()
                            p.op("vector", lambda e: e.tensor_scalar_mul(
                                sc("c"), sc("rg"), 127.0), waits=[], inc="vdn")
                            p.op("vector", lambda e: e.tensor_scalar_mul(
                                sc("sx"), sc("gclp"), 1.0 / 127.0),
                                waits=[], inc="vdn")
                            p.vbub()
                            p.op("vector", lambda e, t=ti: e.tensor_scalar(
                                sxs_stash[:, t:t + 1], sc("sx"),
                                sval_bc[0][:], None, ALU.mult),
                                waits=[THRLD], inc="vdn")
                            p.op("vector", lambda e: e.tensor_scalar(
                                f2[:], f4[:], sc("c"), None, ALU.mult),
                                waits=[], inc="vdn")
                            p.op("vector", lambda e: e.tensor_copy(
                                i32t[:], f2[:]), waits=[], inc="vdn")
                            p.op("vector", lambda e: e.tensor_copy(
                                f5[:], i32t[:]), waits=[], inc="vdn")
                            p.op("vector", lambda e: e.tensor_sub(
                                f1[:], f2[:], f5[:]), waits=[], inc="vdn")
                            p.op("vector", lambda e: e.tensor_single_scalar(
                                f4[:], f1[:], 0.5, ALU.is_ge),
                                waits=[], inc="vdn")
                            p.op("vector", lambda e: e.tensor_add(
                                f5[:], f5[:], f4[:]), waits=[], inc="vdn")
                            p.op("vector", lambda e: e.tensor_single_scalar(
                                f4[:], f1[:], -0.5, ALU.is_lt),
                                waits=[], inc="vdn")
                            p.op("vector", lambda e: e.tensor_sub(
                                f5[:], f5[:], f4[:]), waits=[], inc="vdn")
                            QED = p.w("vdn")
                            for dc in range(16):
                                bk = banks[dc % 2]
                                p.op("tensor",
                                     lambda e, d=dc, b=bk: e.transpose(
                                         b[:, 0:128],
                                         f5[:, d * 128:(d + 1) * 128],
                                         ident[:]),
                                     waits=[QED, p.w("vdn")], inc="tdn")
                                p.op("vector",
                                     lambda e, d=dc, b=bk, t=ti: e.tensor_scalar(
                                         xqT[:, d * 2048 + t * 128:
                                             d * 2048 + (t + 1) * 128],
                                         b[:, 0:128], 127.0, -127.0,
                                         ALU.min, ALU.max),
                                     waits=[p.w("tdn")], inc="vdn")
                    sxv = p.w("vdn")
                    p.op("tensor", lambda e: e.transpose(
                        banks[0][0:16, 0:128], sxs_stash[:], ident[:]),
                        waits=[sxv], inc="tdn")
                    p.op("vector", lambda e: e.tensor_copy(
                        sxrow[:], banks[0][0:16, 0:128]),
                        waits=[p.w("tdn")], inc="vdn")
                    p.dma("gpsimd", "gld", vecs2[:], sxrow[:],
                          waits=[p.w("vdn")])
                    p.fence()

        if True:
                gctx = ExitStack()
                with gctx:
                    gsb = lambda name, shape, dt: gctx.enter_context(
                        nc.sbuf_tensor(name, shape, dt))
                    sxg = [gsb(f"sxg{i}", [128, 512], F32) for i in range(2)]
                    gam_rep = gsb("gam_rep", [128, 2048], F32)
                    hfhi = gsb("hfhi", [128, 8 * 512], BF16)
                    hflo = gsb("hflo", [128, 8 * 512], BF16)
                    oacc = gsb("oacc", [128, 4 * 2048], F32)
                    wub = [gsb(f"wub{i}", [128, 16 * 128], BF16)
                           for i in range(2)]
                    wdnb = [gsb(f"wdnb{i}", [128, 2048], BF16)
                            for i in range(2)]
                    g1 = gsb("g1", [128, 2048], F32)
                    g2 = gsb("g2", [128, 2048], F32)
                    sfu = gsb("sfu", [128, 512], F32)
                    hbuf = gsb("hbuf", [128, 512], F32)
                    x1r = [gsb(f"x1r{i}", [128, 2048], F32) for i in range(2)]
                    gps = gctx.enter_context(
                        nc.psum_tensor("gps", [128, 1024], F32))

                    p.dma("sync", "qld", gam_rep[:],
                          gam_h.partition_broadcast(128), waits=[])
                    GAMLD = p.w("qld")

                    for tg in range(4):
                        p.dma("sync", "qld", sxg[tg % 2][:],
                              vecs_h[:, tg * 512:(tg + 1) * 512]
                              .partition_broadcast(128),
                              waits=[p.w("vdn")])
                        SXLD = p.w("qld")
                        for e8 in range(8):
                            for s in range(8):
                                hsub = e8 * 8 + s
                                p.dma("sync", "qld", wub[s % 2][:],
                                      wupT3[:, :, hsub * 128:(hsub + 1) * 128],
                                      waits=[p.w("tdn")])
                                wl = p.w("qld")
                                upb = banks[4 + s % 2]
                                pe = p.w("vdn")
                                for dc in range(16):
                                    p.op("tensor",
                                         lambda e, w2=wub[s % 2], d=dc, b=upb,
                                         f=(dc == 0), l=(dc == 15), t0=tg:
                                         e.matmul(
                                             b[:],
                                             w2[:, d * 128:(d + 1) * 128],
                                             xqT[:, d * 2048 + t0 * 512:
                                                 d * 2048 + (t0 + 1) * 512],
                                             start=f, stop=l),
                                         waits=[wl, pe],
                                         inc="tdn" if dc == 15 else None)
                                um = p.w("tdn")
                                p.op("vector",
                                     lambda e, b=upb, t0=tg: e.tensor_mul(
                                         sfu[:], b[:], sxg[t0 % 2][:]),
                                     waits=[um, SXLD], inc="vdn")
                                vd = p.w("vdn")
                                p.op("scalar",
                                     lambda e: e.activation(
                                         hbuf[:], sfu[:], AF.Silu),
                                     waits=[vd], inc="adn")
                                ad = p.w("adn")
                                p.op("vector",
                                     lambda e, s2=s: e.tensor_copy(
                                         hfhi[:, s2 * 512:(s2 + 1) * 512],
                                         hbuf[:]),
                                     waits=[ad, p.w("tdn")], inc="vdn")
                                p.op("vector",
                                     lambda e, s2=s: e.scalar_tensor_tensor(
                                         hflo[:, s2 * 512:(s2 + 1) * 512],
                                         hbuf[:], 1.0,
                                         hfhi[:, s2 * 512:(s2 + 1) * 512],
                                         ALU.mult, ALU.subtract),
                                     waits=[], inc="vdn")
                            HFD = p.w("vdn")
                            for half in range(2):
                                for s in range(8):
                                    hsub = e8 * 8 + s
                                    p.dma("sync", "qld", wdnb[s % 2][:],
                                          wdnT_h[hsub * 128:(hsub + 1) * 128, :],
                                          waits=[p.w("tdn")])
                                    wl = p.w("qld")
                                    pe = p.w("vdn")
                                    for t2 in range(2):
                                        tt = half * 2 + t2
                                        for ct in range(4):
                                            gi = t2 * 4 + ct
                                            dst = (banks[gi][:] if gi < 6 else
                                                   gps[:, (gi - 6) * 512:
                                                       (gi - 5) * 512])
                                            for hilo in range(2):
                                                hf = hfhi if hilo == 0 else hflo
                                                st = s == 0 and hilo == 0
                                                sp = s == 7 and hilo == 1
                                                p.op("tensor",
                                                     lambda e, h3=hf, s2=s,
                                                     t3=tt, c=ct, d=dst,
                                                     w2=wdnb[s % 2], f=st, l=sp:
                                                     e.matmul(
                                                         d,
                                                         h3[:, s2 * 512 + t3 * 128:
                                                            s2 * 512 + (t3 + 1) * 128],
                                                         w2[:, c * 512:(c + 1) * 512],
                                                         start=f, stop=l),
                                                     waits=[HFD, wl, pe],
                                                     inc="tdn"
                                                     if (hilo == 1 and
                                                         ct == 3 and t2 == 1)
                                                     else None)
                                dmm = p.w("tdn")
                                for t2 in range(2):
                                    tt = half * 2 + t2
                                    for ct in range(4):
                                        gi = t2 * 4 + ct
                                        src = (banks[gi][:] if gi < 6 else
                                               gps[:, (gi - 6) * 512:
                                                   (gi - 5) * 512])
                                        col = tt * 2048 + ct * 512
                                        if e8 == 0:
                                            p.op("vector",
                                                 lambda e, s2=src, c2=col:
                                                 e.tensor_copy(
                                                     oacc[:, c2:c2 + 512], s2),
                                                 waits=[dmm], inc="vdn")
                                        else:
                                            p.op("vector",
                                                 lambda e, s2=src, c2=col:
                                                 e.scalar_tensor_tensor(
                                                     oacc[:, c2:c2 + 512], s2,
                                                     1.0, oacc[:, c2:c2 + 512],
                                                     ALU.mult, ALU.add),
                                                 waits=[dmm], inc="vdn")
                        for tt in range(4):
                            ti = tg * 4 + tt
                            p.dma("gpsimd", "gld", x1r[tt % 2][:],
                                  x1_h[ti * 128:(ti + 1) * 128, :],
                                  waits=[p.w("vdn")])
                            X1L = p.w("gld")
                            p.op("vector",
                                 lambda e, t3=tt: e.tensor_mul(
                                     g1[:], oacc[:, t3 * 2048:(t3 + 1) * 2048],
                                     gam_rep[:]),
                                 waits=[GAMLD, p.w("gld")], inc="vdn")
                            p.op("vector", lambda e: e.tensor_scalar(
                                g2[:], g1[:], sval_bc[1][:], None, ALU.mult),
                                waits=[], inc="vdn")
                            p.op("vector",
                                 lambda e, t3=tt: e.scalar_tensor_tensor(
                                     g1[:], g2[:], 1.0, x1r[t3 % 2][:],
                                     ALU.mult, ALU.add, accum_out=sc("sums")),
                                 waits=[X1L], inc="vdn")
                            p.op("vector", lambda e: e.scalar_tensor_tensor(
                                g2[:], g1[:], 1.0, g1[:], ALU.mult, ALU.mult,
                                accum_out=sc("ssq")),
                                waits=[], inc="vdn")
                            p.vbub()
                            p.op("vector", lambda e: e.tensor_scalar_mul(
                                sc("nm"), sc("sums"), -1.0 / D),
                                waits=[], inc="vdn")
                            p.op("vector", lambda e: e.tensor_scalar_mul(
                                sc("t1"), sc("ssq"), 1.0 / D),
                                waits=[], inc="vdn")
                            p.vbub()
                            p.op("vector", lambda e: e.tensor_mul(
                                sc("m2t"), sc("nm"), sc("nm")),
                                waits=[], inc="vdn")
                            p.vbub()
                            p.op("vector", lambda e: e.tensor_sub(
                                sc("var"), sc("t1"), sc("m2t")),
                                waits=[], inc="vdn")
                            p.vbub()
                            p.vbub(inc="vdn")
                            vd = p.w("vdn")
                            p.op("scalar", lambda e: e.activation(
                                sc("sd"), sc("var"), AF.Sqrt, bias=epsln[:]),
                                waits=[vd], inc="adn")
                            ad = p.w("adn")
                            p.vbub(waits=[ad])
                            p.vbub()
                            p.op("vector", lambda e: e.reciprocal(
                                sc("rstd"), sc("sd")), waits=[], inc="vdn")
                            p.vbub()
                            p.op("vector", lambda e: e.tensor_scalar(
                                g2[:], g1[:], sc("nm"), None, ALU.add),
                                waits=[], inc="vdn")
                            p.op("vector", lambda e: e.tensor_scalar(
                                g1[:], g2[:], sc("rstd"), None, ALU.mult),
                                waits=[], inc="vdn")
                            p.dma("gpsimd", "gld",
                                  out_h[ti * 128:(ti + 1) * 128, :], g1[:],
                                  waits=[p.w("vdn")])
                    p.fence()

        with nc.Block() as block:
            p.emit(block, sems)

    return nc


_NC_CACHE = [None]


def kernel(L0_attn_x, L0_attn_Wq, L0_attn_Wk, L0_attn_Wv, L0_attn_Wo,
           L0_ffn_weight_up, L0_ffn_weight_down, L0_ffn_norm_weight,
           L0_ffn_gamma):
    x = np.ascontiguousarray(np.asarray(L0_attn_x, dtype=np.float32))
    wq = np.ascontiguousarray(np.asarray(L0_attn_Wq, dtype=np.float32))
    wk = np.ascontiguousarray(np.asarray(L0_attn_Wk, dtype=np.float32))
    wv = np.ascontiguousarray(np.asarray(L0_attn_Wv, dtype=np.float32))
    wo = np.ascontiguousarray(np.asarray(L0_attn_Wo, dtype=np.float32))
    wup = np.ascontiguousarray(np.asarray(L0_ffn_weight_up, dtype=np.float32))
    wdn = np.ascontiguousarray(np.asarray(L0_ffn_weight_down, dtype=np.float32))
    nw = np.ascontiguousarray(
        np.asarray(L0_ffn_norm_weight, dtype=np.float32).reshape(1, D))
    gam = np.ascontiguousarray(
        np.asarray(L0_ffn_gamma, dtype=np.float32).reshape(1, D))

    if _NC_CACHE[0] is None:
        _NC_CACHE[0] = build()
    nc = _NC_CACHE[0]

    in_maps = []
    for c in range(8):
        in_maps.append({
            "x": np.ascontiguousarray(x[c]),
            "wq": wq, "wk": wk, "wv": wv, "wo": wo,
            "wup": wup, "wdn": wdn, "nw": nw, "gam": gam,
        })
    res = run_bass_kernel_spmd(nc, in_maps, core_ids=list(range(8)))
    return np.stack([r["out"] for r in res.results], axis=0)


if __name__ == "__main__":
    nc = build()
    print("build ok")


# version 03
# speedup vs baseline: 1.6137x; 1.6137x over previous
import math
import numpy as np
import concourse.bass as bass
import concourse.mybir as mybir
from concourse.bass_utils import run_bass_kernel_spmd
from contextlib import ExitStack

F32 = mybir.dt.float32
BF16 = mybir.dt.bfloat16
I32 = mybir.dt.int32
AF = mybir.ActivationFunctionType
ALU = mybir.AluOpType
AX = mybir.AxisListType

T = 2048
D = 2048
FF = 8192
NT = 16
SQ = math.sqrt(128.0)
ENGS = ("tensor", "vector", "scalar", "gpsimd", "sync")


def slope(h):
    return 2.0 ** (-(h // 4 + 1) / 2.0)


class Prog:
    def __init__(self):
        self.items = {e: [] for e in ENGS}
        self.cnt = {}
        self.res = {}

    def now(self, sem):
        return self.cnt.get(sem, 0)

    def w(self, sem):
        return (sem, self.now(sem))

    def use(self, name):
        return self.res.get(name, ("gdn", 0))

    def done(self, name, sem):
        self.res[name] = (sem, self.now(sem))

    def op(self, eng, fn, waits=(), inc=None, amt=1):
        self.items[eng].append((list(waits), fn, inc, amt))
        if inc is not None:
            self.cnt[inc] = self.cnt.get(inc, 0) + amt

    def vbub(self, waits=(), inc=None):
        self.op("vector",
                lambda e: e.tensor_copy(self.bubA[:], self.bubB[:]),
                waits, inc=inc)

    def dma(self, qeng, sem, out, in_, waits=()):
        self.op(qeng, lambda e, o=out, i=in_: e.dma_start(out=o, in_=i),
                waits, inc=sem, amt=16)

    def fence(self):
        snap = [(s, v) for s, v in self.cnt.items()]
        for e in ENGS:
            self.items[e].append((snap, None, None, 0))

    def emit(self, block, sems):
        def run(eng_obj, name):
            emitted = {}
            for waits, fn, inc, amt in self.items[name]:
                for s, v in waits:
                    if v > 0 and emitted.get(s, -1) < v:
                        eng_obj.wait_ge(sems[s], v)
                        emitted[s] = v
                if fn is not None:
                    instr = fn(eng_obj)
                    if inc is not None:
                        instr.then_inc(sems[inc], amt)

        @block.tensor
        def _(e):
            run(e, "tensor")

        @block.vector
        def _(e):
            run(e, "vector")

        @block.scalar
        def _(e):
            run(e, "scalar")

        @block.gpsimd
        def _(e):
            run(e, "gpsimd")

        @block.sync
        def _(e):
            run(e, "sync")


def build():
    nc = bass.Bass("TRN2", target_bir_lowering=False, debug=False)

    x_h = nc.dram_tensor("x", [T, D], F32, kind="ExternalInput").ap()
    wq_h = nc.dram_tensor("wq", [D, D], F32, kind="ExternalInput").ap()
    wk_h = nc.dram_tensor("wk", [D, 512], F32, kind="ExternalInput").ap()
    wv_h = nc.dram_tensor("wv", [D, 512], F32, kind="ExternalInput").ap()
    wo_h = nc.dram_tensor("wo", [D, D], F32, kind="ExternalInput").ap()
    wup_h = nc.dram_tensor("wup", [FF, D], F32, kind="ExternalInput").ap()
    wdn_h = nc.dram_tensor("wdn", [D, FF], F32, kind="ExternalInput").ap()
    nw_h = nc.dram_tensor("nw", [1, D], F32, kind="ExternalInput").ap()
    gam_h = nc.dram_tensor("gam", [1, D], F32, kind="ExternalInput").ap()
    out_h = nc.dram_tensor("out", [T, D], F32, kind="ExternalOutput").ap()

    xT_h = nc.dram_tensor("xTs", [D, T], F32, kind="Internal").ap()
    yT_h = nc.dram_tensor("yTs", [D, T], F32, kind="Internal").ap()
    x1_h = nc.dram_tensor("x1s", [T, D], F32, kind="Internal").ap()
    wupT_h = nc.dram_tensor("wupT", [D, FF], BF16, kind="Internal").ap()
    wdnT_h = nc.dram_tensor("wdnT", [FF, D], BF16, kind="Internal").ap()
    vecs_h = nc.dram_tensor("vecs", [1, T], F32, kind="Internal").ap()
    scal_h = nc.dram_tensor("scal", [1, 2], F32, kind="Internal").ap()

    xT3 = xT_h.rearrange("(c p) t -> p c t", p=128)
    wupT3 = wupT_h.rearrange("(c p) h -> p c h", p=128)
    wdnT3 = wdnT_h.rearrange("(b p) c -> p b c", p=128)
    vecs2 = vecs_h.rearrange("a (b c) -> (a b) c", b=16)

    p = Prog()
    SEMS = ("qld", "gld", "tdn", "vdn", "adn", "gdn")

    ctx = ExitStack()
    with ctx:
        sb = lambda name, shape, dt: ctx.enter_context(nc.sbuf_tensor(name, shape, dt))
        sems = {s: ctx.enter_context(nc.semaphore(s)) for s in SEMS}

        ident = sb("ident", [128, 128], F32)
        identb = sb("identb", [128, 128], BF16)
        p.bubA = sb("bubA", [128, 1], F32)
        p.bubB = sb("bubB", [128, 1], F32)
        ones = sb("ones", [128, 1], F32)
        epsln = sb("epsln", [128, 1], F32)
        epsrms = sb("epsrms", [128, 1], F32)
        STNAMES = ["m0", "m1", "bias0", "bias1", "sig0", "sig1", "r0", "r1",
                   "sp0", "sp1", "sp2", "sp3", "sums", "nm", "ssq", "t1",
                   "m2t", "var", "sd", "rstd", "gmax", "gclp", "rg", "c",
                   "sx", "tstat", "waccu", "waccd"]
        stt = {n: sb("st_" + n, [128, 1], F32) for n in STNAMES}
        mean_bc = [sb(f"mean_bc{i}", [128, 1], F32) for i in range(2)]
        sval_bc = [sb(f"sval_bc{i}", [128, 1], F32) for i in range(2)]
        thrp_bc = [sb(f"thrp_bc{i}", [128, 1], F32) for i in range(2)]
        thrn_bc = [sb(f"thrn_bc{i}", [128, 1], F32) for i in range(2)]
        mean_sb = sb("mean_sb", [1, 2], F32)
        sxs_stash = sb("sxs_stash", [128, 16], F32)
        sxrow = sb("sxrow", [16, 128], F32)
        banks = [ctx.enter_context(nc.psum_tensor(f"pb{i}", [128, 512], F32))
                 for i in range(6)]

        PARN = ("m", "bias", "sig", "r")

        def sc(name, par=0):
            key = f"{name}{par}" if name in PARN else name
            return stt[key][:]

        def gop(fn, waits=()):
            p.op("gpsimd", fn, waits, inc="gdn")

        gop(lambda e: e.memset(ones[:], 1.0))
        gop(lambda e: e.memset(p.bubB[:], 0.0))
        gop(lambda e: e.memset(epsln[:], 1e-5))
        gop(lambda e: e.memset(epsrms[:], 1e-6))
        for t_ in (ident, identb):
            gop(lambda e, t2=t_: e.memset(t2[:], 1.0))
            gop(lambda e, t2=t_: e.affine_select(
                t2[:], t2[:], [[-1, 128]], ALU.is_ge, 0.0,
                base=0, channel_multiplier=1))
            gop(lambda e, t2=t_: e.affine_select(
                t2[:], t2[:], [[1, 128]], ALU.is_ge, 0.0,
                base=0, channel_multiplier=-1))

        bf_ctx = ExitStack()
        with bf_ctx:
            wstage = [bf_ctx.enter_context(
                nc.sbuf_tensor(f"wst{i}", [128, 2048], F32)) for i in range(4)]

            actx = ExitStack()
            with actx:
                asb = lambda name, shape, dt: actx.enter_context(
                    nc.sbuf_tensor(name, shape, dt))
                Ccol = asb("Ccol", [128, T], F32)
                maskv = asb("maskv", [128, 4 * 512], F32)
                K_all = asb("K_all", [128, 4 * T], F32)
                V_all = asb("V_all", [128, 16 * 512], F32)
                Qpair = asb("Qpair", [128, 2 * T], F32)
                Ssb = [asb(f"Ssb{i}", [128, T], F32) for i in range(2)]
                Psb = asb("Psb", [128, T], F32)
                dtmp = asb("dtmp", [128, 512], F32)
                PTb = asb("PTb", [128, 16 * 256], F32)
                yh = [asb(f"yh{i}", [128, T], F32) for i in range(2)]
                wkvst = [asb(f"wkv{i}", [128, 512], F32) for i in range(2)]
                wqst = [asb(f"wqst{i}", [128, 256], F32) for i in range(2)]
                estage = [asb(f"est{i}", [128, 1024], F32) for i in range(2)]
                ptp = actx.enter_context(
                    nc.psum_tensor("ptp", [128, 1024], F32))
                ptp3 = ptp[:].rearrange("p (a b) -> p a b", a=2)
                PTb3 = PTb[:].rearrange("p (k ab) -> p k ab", ab=256)

                gop(lambda e: e.memset(K_all[:], 0.0))
                gop(lambda e: e.memset(V_all[:], 0.0))
                gop(lambda e: e.iota(Ccol[:], [[1, T]], base=0,
                                     channel_multiplier=0,
                                     allow_small_or_imprecise_dtypes=True))
                for v in range(4):
                    gop(lambda e, v=v: e.memset(
                        maskv[:, v * 512:(v + 1) * 512], 0.0))
                    gop(lambda e, v=v: e.affine_select(
                        maskv[:, v * 512:(v + 1) * 512],
                        maskv[:, v * 512:(v + 1) * 512],
                        [[-1, 512]], ALU.is_ge, -1e33,
                        base=128 * v, channel_multiplier=1))
                A_DONE = p.w("gdn")

                for ti in range(NT):
                    slot = wstage[ti % 2]
                    p.dma("gpsimd", "gld", slot[:],
                          x_h[ti * 128:(ti + 1) * 128, :],
                          waits=[p.use(f"wst{ti % 2}"), A_DONE])
                    ld = p.w("gld")
                    half = PTb[:, (ti % 2) * 2048:(ti % 2) * 2048 + 2048]
                    for dc in range(16):
                        bk = banks[dc % 2]
                        p.op("tensor",
                             lambda e, s=slot, d=dc, b=bk: e.transpose(
                                 b[:, 0:128], s[:, d * 128:(d + 1) * 128],
                                 ident[:]),
                             waits=[ld, p.use(f"bank{dc % 2}"), A_DONE],
                             inc="tdn")
                        p.op("vector",
                             lambda e, b=bk, hf=half, d=dc: e.tensor_copy(
                                 hf[:, d * 128:(d + 1) * 128], b[:, 0:128]),
                             waits=[p.w("tdn"), p.use(f"ptbh{ti % 2}")],
                             inc="vdn")
                        p.done(f"bank{dc % 2}", "vdn")
                    p.done(f"wst{ti % 2}", "tdn")
                    p.dma("gpsimd", "gld", xT3[:, :, ti * 128:(ti + 1) * 128],
                          half.rearrange("p (c t) -> p c t", c=16),
                          waits=[p.w("vdn")])
                    p.done(f"ptbh{ti % 2}", "gld")
                p.fence()

                for pi in range(4):
                    kv = pi
                    for dc in range(16):
                        si = dc % 3
                        xslot = wstage[si]
                        p.dma("sync", "qld", xslot[:],
                              xT_h[dc * 128:(dc + 1) * 128, :],
                              waits=[p.use(f"wst{si}")])
                        wslot = wkvst[dc % 2]
                        p.dma("sync", "qld", wslot[:],
                              wv_h[dc * 128:(dc + 1) * 128, :],
                              waits=[p.use(f"wkv{dc % 2}")])
                        qslot = wqst[dc % 2]
                        p.dma("sync", "qld", qslot[:, 0:128],
                              wk_h[dc * 128:(dc + 1) * 128,
                                   kv * 128:(kv + 1) * 128],
                              waits=[p.use(f"wqst{dc % 2}")])
                        ld = p.w("qld")
                        first, last = dc == 0, dc == 15
                        for kt in range(4):
                            bw = [p.use(f"bank{kt}")] if first else []
                            p.op("tensor",
                                 lambda e, q=qslot, xs=xslot, k=kt, f=first,
                                 l=last: e.matmul(
                                     banks[k][:], q[:, 0:128],
                                     xs[:, k * 512:(k + 1) * 512],
                                     start=f, stop=l),
                                 waits=[ld] + bw)
                        for vt in range(4):
                            ti = pi * 4 + vt
                            bname = (f"bank{4 + vt}" if vt < 2
                                     else f"ptph{vt - 2}")
                            dst = (banks[4][:] if vt == 0 else banks[5][:]
                                   if vt == 1 else
                                   ptp[:, (vt - 2) * 512:(vt - 1) * 512])
                            bw = [p.use(bname)] if first else []
                            p.op("tensor",
                                 lambda e, xs=xslot, w2=wslot, t=ti, d=dst,
                                 f=first, l=last: e.matmul(
                                     d, xs[:, t * 128:(t + 1) * 128],
                                     w2[:], start=f, stop=l),
                                 waits=[ld] + bw,
                                 inc="tdn" if vt == 3 else None)
                        p.done(f"wst{si}", "tdn")
                        p.done(f"wkv{dc % 2}", "tdn")
                        p.done(f"wqst{dc % 2}", "tdn")
                    mmd = p.w("tdn")
                    for kt in range(4):
                        p.op("vector",
                             lambda e, k=kt, kvv=kv: e.tensor_copy(
                                 K_all[:, kvv * T + k * 512:
                                       kvv * T + (k + 1) * 512],
                                 banks[k][:]),
                             waits=[mmd], inc="vdn")
                        p.done(f"bank{kt}", "vdn")
                    for vt in range(4):
                        ti = pi * 4 + vt
                        bname = f"bank{4 + vt}" if vt < 2 else f"ptph{vt - 2}"
                        src = (banks[4][:] if vt == 0 else banks[5][:]
                               if vt == 1 else
                               ptp[:, (vt - 2) * 512:(vt - 1) * 512])
                        p.op("vector",
                             lambda e, s=src, t=ti: e.tensor_copy(
                                 V_all[:, t * 512:(t + 1) * 512], s),
                             waits=[mmd], inc="vdn")
                        p.done(bname, "vdn")
                p.fence()

                def emit_e1_chunk(ch):
                    si = ch % 2
                    slot = estage[si]
                    if ch < 128:
                        src = wup_h[(ch // 2) * 128:(ch // 2 + 1) * 128,
                                    (ch % 2) * 1024:(ch % 2 + 1) * 1024]
                        acc = sc("waccu")
                    else:
                        c2 = ch - 128
                        cc, qq = c2 // 8, c2 % 8
                        src = wdn_h[cc * 128:(cc + 1) * 128,
                                    qq * 1024:(qq + 1) * 1024]
                        acc = sc("waccd")
                    p.dma("gpsimd", "gld", slot[:], src,
                          waits=[p.use(f"est{si}")])
                    ld = p.w("gld")
                    p.op("vector",
                         lambda e, s=slot: e.reduce_sum(
                             sc("tstat"), s[:], axis=AX.X,
                             apply_absolute_value=True),
                         waits=[ld], inc="vdn")
                    p.done(f"est{si}", "vdn")
                    p.vbub()
                    if ch % 128 == 0:
                        p.op("vector",
                             lambda e, a=acc: e.tensor_copy(a, sc("tstat")),
                             waits=[], inc="vdn")
                    else:
                        p.op("vector",
                             lambda e, a=acc: e.scalar_tensor_tensor(
                                 a, sc("tstat"), 1.0, a, ALU.mult, ALU.add),
                             waits=[], inc="vdn")

                for pr in range(8):
                    for dc in range(16):
                        si = dc % 3
                        xslot = wstage[si]
                        p.dma("sync", "qld", xslot[:],
                              xT_h[dc * 128:(dc + 1) * 128, :],
                              waits=[p.use(f"wst{si}")])
                        qslot = wqst[dc % 2]
                        p.dma("sync", "qld", qslot[:],
                              wq_h[dc * 128:(dc + 1) * 128,
                                   pr * 256:(pr + 1) * 256],
                              waits=[p.use(f"wqst{dc % 2}")])
                        ld = p.w("qld")
                        first, last = dc == 0, dc == 15
                        for gi in range(8):
                            hh, tg = gi // 4, gi % 4
                            bname = f"bank{gi}" if gi < 6 else f"ptph{gi - 6}"
                            dst = (banks[gi][:] if gi < 6
                                   else ptp[:, (gi - 6) * 512:(gi - 5) * 512])
                            bw = [p.use(bname)] if first else []
                            p.op("tensor",
                                 lambda e, q=qslot, xs=xslot, h2=hh, t=tg,
                                 d=dst, f=first, l=last: e.matmul(
                                     d, q[:, h2 * 128:(h2 + 1) * 128],
                                     xs[:, t * 512:(t + 1) * 512],
                                     start=f, stop=l),
                                 waits=[ld] + bw,
                                 inc="tdn" if gi == 7 else None)
                        p.done(f"wst{si}", "tdn")
                        p.done(f"wqst{dc % 2}", "tdn")
                    qmm = p.w("tdn")
                    for gi in range(8):
                        hh, tg = gi // 4, gi % 4
                        scl = 1.0 / (SQ * slope(2 * pr + hh))
                        bname = f"bank{gi}" if gi < 6 else f"ptph{gi - 6}"
                        src = (banks[gi][:] if gi < 6
                               else ptp[:, (gi - 6) * 512:(gi - 5) * 512])
                        p.op("vector",
                             lambda e, s=src, h2=hh, t=tg, sc2=scl:
                             e.tensor_scalar_mul(
                                 Qpair[:, h2 * T + t * 512:
                                       h2 * T + (t + 1) * 512], s, sc2),
                             waits=[qmm, p.use("Qpair")], inc="vdn")
                        p.done(bname, "vdn")
                    QEV = p.w("vdn")

                    for hh in range(2):
                        h = 2 * pr + hh
                        kv = h // 4
                        s_h = slope(h)

                        def emit_scores(qi):
                            par = qi % 2
                            nkt = qi // 4 + 1
                            kpad = 512 * nkt
                            dkt = qi // 4
                            mvar = qi % 4
                            smm = []
                            for kt in range(nkt):
                                p.op("tensor",
                                     lambda e, k=kt, q=qi, h2=hh, kvv=kv:
                                     e.matmul(
                                         banks[k][:],
                                         Qpair[:, h2 * T + q * 128:
                                               h2 * T + (q + 1) * 128],
                                         K_all[:, kvv * T + k * 512:
                                               kvv * T + (k + 1) * 512],
                                         start=True, stop=True),
                                     waits=[QEV, p.use(f"bank{kt}")],
                                     inc="tdn")
                                smm.append(p.w("tdn"))
                            for kt in range(nkt):
                                if kt == dkt:
                                    p.op("vector",
                                         lambda e, k=kt: e.tensor_add(
                                             dtmp[:], banks[k][:],
                                             Ccol[:, k * 512:(k + 1) * 512]),
                                         waits=[smm[kt], p.use(f"Ssb{par}")],
                                         inc="vdn")
                                    p.op("vector",
                                         lambda e, k=kt, pa=par, v=mvar:
                                         e.tensor_add(
                                             Ssb[pa][:, k * 512:(k + 1) * 512],
                                             dtmp[:],
                                             maskv[:, v * 512:(v + 1) * 512]),
                                         waits=[], inc="vdn")
                                else:
                                    p.op("vector",
                                         lambda e, k=kt, pa=par: e.tensor_add(
                                             Ssb[pa][:, k * 512:(k + 1) * 512],
                                             banks[k][:],
                                             Ccol[:, k * 512:(k + 1) * 512]),
                                         waits=[smm[kt], p.use(f"Ssb{par}")],
                                         inc="vdn")
                                p.done(f"bank{kt}", "vdn")
                            p.op("vector",
                                 lambda e, pa=par, kp=kpad: e.reduce_max(
                                     sc("m", pa), Ssb[pa][:, 0:kp], axis=AX.X),
                                 waits=[], inc="vdn")
                            p.vbub()
                            p.op("vector",
                                 lambda e, pa=par, s2=s_h: e.tensor_scalar_mul(
                                     sc("bias", pa), sc("m", pa), -s2),
                                 waits=[], inc="vdn")
                            p.vbub()
                            p.vbub(inc="vdn")
                            vd = p.w("vdn")
                            p.op("scalar",
                                 lambda e, pa=par, kp=kpad, s2=s_h:
                                 e.activation(
                                     Psb[:, 0:kp], Ssb[pa][:, 0:kp], AF.Exp,
                                     bias=sc("bias", pa), scale=s2,
                                     accum_out=sc("sig", pa)),
                                 waits=[vd], inc="adn")
                            ad = p.w("adn")
                            p.vbub(waits=[ad])
                            p.vbub()
                            p.op("vector",
                                 lambda e, pa=par: e.reciprocal(
                                     sc("r", pa), sc("sig", pa)),
                                 waits=[], inc="vdn")
                            p.vbub()
                            p.op("vector",
                                 lambda e, pa=par, kp=kpad: e.tensor_scalar(
                                     Ssb[pa][:, 0:kp], Psb[:, 0:kp],
                                     sc("r", pa), None, ALU.mult),
                                 waits=[], inc="vdn")

                        def emit_pt_pv(qi):
                            par = qi % 2
                            nkt = qi // 4 + 1
                            kpad = 512 * nkt
                            nkb = kpad // 128
                            pnd = p.w("vdn")
                            for kb0 in range(0, nkb, 2):
                                n2 = min(2, nkb - kb0)
                                for j in range(n2):
                                    p.op("tensor",
                                         lambda e, pa=par, k=kb0 + j, hl=j:
                                         e.transpose(
                                             ptp[:, hl * 512:hl * 512 + 128],
                                             Ssb[pa][:, k * 128:(k + 1) * 128],
                                             ident[:]),
                                         waits=[pnd, p.use(f"ptph{j}"),
                                                p.use("PTb")],
                                         inc="tdn")
                                td = p.w("tdn")
                                off = (qi % 2) * 128
                                p.op("vector",
                                     lambda e, k0=kb0, n=n2, o=off:
                                     e.tensor_copy(
                                         PTb3[:, k0:k0 + n, o:o + 128],
                                         ptp3[:, 0:n, 0:128]),
                                     waits=[td], inc="vdn")
                                p.done("ptph0", "vdn")
                                if n2 == 2:
                                    p.done("ptph1", "vdn")
                            p.done(f"Ssb{par}", "tdn")
                            if qi % 2 == 1:
                                qg = qi // 2
                                nkc = kpad // 128
                                ptev = p.w("vdn")
                                for kc in range(nkc):
                                    p.op("tensor",
                                         lambda e, k=kc, kvv=kv,
                                         f=(kc == 0), l=(kc == nkc - 1):
                                         e.matmul(
                                             banks[4][:, 0:256],
                                             V_all[:, k * 512 + kvv * 128:
                                                   k * 512 + (kvv + 1) * 128],
                                             PTb[:, k * 256:(k + 1) * 256],
                                             start=f, stop=l),
                                         waits=[ptev, p.use("bank4")],
                                         inc="tdn" if kc == nkc - 1 else None)
                                p.done("PTb", "tdn")
                                pvd = p.w("tdn")
                                p.op("vector",
                                     lambda e, g=qg, h2=hh: e.tensor_copy(
                                         yh[h2][:, g * 256:(g + 1) * 256],
                                         banks[4][:, 0:256]),
                                     waits=[pvd, p.use(f"yh{hh}")],
                                     inc="vdn")
                                p.done("bank4", "vdn")

                        for qi in range(NT):
                            emit_scores(qi)
                            if qi >= 1:
                                emit_pt_pv(qi - 1)
                            emit_e1_chunk(pr * 32 + hh * 16 + qi)
                        emit_pt_pv(NT - 1)
                        p.dma("gpsimd", "gld", yT_h[h * 128:(h + 1) * 128, :],
                              yh[hh][:], waits=[p.w("vdn")])
                        p.done(f"yh{hh}", "gld")
                        p.done("Qpair", "tdn")
                p.fence()

                for wi in range(2):
                    acc = sc("waccu") if wi == 0 else sc("waccd")
                    p.vbub()
                    p.vbub(inc="vdn")
                    vd = p.w("vdn")
                    p.op("tensor",
                         lambda e, a=acc: e.matmul(banks[0][0:1, 0:1], a,
                                                   ones[:], start=True,
                                                   stop=True),
                         waits=[vd], inc="tdn")
                    td = p.w("tdn")
                    p.op("vector",
                         lambda e, w2=wi: e.tensor_scalar_mul(
                             mean_sb[0:1, w2:w2 + 1], banks[0][0:1, 0:1],
                             1.0 / (FF * D)),
                         waits=[td], inc="vdn")
                p.dma("gpsimd", "gld", scal_h[:], mean_sb[:],
                      waits=[p.w("vdn")])
                for w2 in range(2):
                    p.dma("sync", "qld", mean_bc[w2][:],
                          scal_h[:, w2:w2 + 1].partition_broadcast(128),
                          waits=[p.w("gld")])
                mbcl = p.w("qld")
                for w2 in range(2):
                    p.op("vector",
                         lambda e, i=w2: e.tensor_scalar_max(
                             sval_bc[i][:], mean_bc[i][:], 1e-8),
                         waits=[mbcl], inc="vdn")
                    p.vbub()
                    p.op("vector",
                         lambda e, i=w2: e.tensor_scalar_mul(
                             thrp_bc[i][:], sval_bc[i][:], 0.5),
                         waits=[], inc="vdn")
                    p.vbub()
                    p.op("vector",
                         lambda e, i=w2: e.tensor_scalar_mul(
                             thrn_bc[i][:], thrp_bc[i][:], -1.0),
                         waits=[], inc="vdn")
                p.vbub()
                p.vbub(inc="vdn")
                THRLD = p.w("vdn")
                p.fence()

            xqT = ctx.enter_context(
                nc.sbuf_tensor("xqT", [128, 16 * 2048], BF16, side="right"))

            e2f_ctx = ExitStack()
            with e2f_ctx:
                esb = lambda name, shape, dt: e2f_ctx.enter_context(
                    nc.sbuf_tensor(name, shape, dt))
                tern1 = esb("tern1", [128, 1024], F32)
                tern2 = esb("tern2", [128, 1024], F32)
                ternf = esb("ternf", [128, 1024], F32)
                tstage = esb("tstage", [128, 8192], BF16)
                estg = [esb(f"e2st{i}", [128, 1024], F32) for i in range(2)]
                xres = [esb(f"xres{i}", [128, 2048], F32) for i in range(2)]
                f1 = esb("f1", [128, 2048], F32)
                f2 = esb("f2", [128, 2048], F32)
                f3 = esb("f3", [128, 2048], F32)
                f4 = esb("f4", [128, 2048], F32)
                f5 = esb("f5", [128, 2048], F32)
                i32t = esb("i32t", [128, 2048], I32)
                nw_rep = esb("nw_rep", [128, 2048], F32)
                fps = e2f_ctx.enter_context(
                    nc.psum_tensor("fps", [128, 1024], F32))

                def fbank(g):
                    return (banks[g][:] if g < 6
                            else fps[:, (g - 6) * 512:(g - 5) * 512])

                p.dma("sync", "qld", nw_rep[:],
                      nw_h.partition_broadcast(128), waits=[])
                NWLD = p.w("qld")

                def emit_e2_unit(u):
                    si = u % 2
                    slot = estg[si]
                    if u < 128:
                        rb, hh2 = u // 2, u % 2
                        src = wup_h[rb * 128:(rb + 1) * 128,
                                    hh2 * 1024:(hh2 + 1) * 1024]
                        wi2 = 0
                        blk_dst = hh2 * 8
                        store = ("up", rb) if hh2 == 1 else None
                    else:
                        c2 = u - 128
                        cc, qq = c2 // 8, c2 % 8
                        src = wdn_h[cc * 128:(cc + 1) * 128,
                                    qq * 1024:(qq + 1) * 1024]
                        wi2 = 1
                        blk_dst = qq * 8
                        store = ("dn", cc) if qq == 7 else None
                    p.dma("gpsimd", "gld", slot[:], src,
                          waits=[p.use(f"e2st{si}")])
                    ldq = p.w("gld")
                    p.op("vector",
                         lambda e, s=slot, i=wi2: e.tensor_scalar(
                             tern1[:], s[:], thrp_bc[i][:], None, ALU.is_ge),
                         waits=[ldq, THRLD, p.use("tern12")], inc="vdn")
                    p.op("vector",
                         lambda e, s=slot, i=wi2: e.tensor_scalar(
                             tern2[:], s[:], thrn_bc[i][:], None, ALU.is_le),
                         waits=[], inc="vdn")
                    p.done(f"e2st{si}", "vdn")
                    p.done("tern12", "vdn")
                    p.op("vector",
                         lambda e: e.tensor_sub(ternf[:], tern1[:], tern2[:]),
                         waits=[p.use("ternf")], inc="vdn")
                    t0 = p.w("vdn")
                    for b in range(8):
                        bi = b % 2
                        p.op("tensor",
                             lambda e, b2=b, i=bi: e.transpose(
                                 banks[i][:, 0:128],
                                 ternf[:, b2 * 128:(b2 + 1) * 128],
                                 ident[:]),
                             waits=[t0, p.use(f"bank{bi}")], inc="tdn")
                        p.op("vector",
                             lambda e, b2=b, i=bi, d=blk_dst: e.tensor_copy(
                                 tstage[:, (d + b2) * 128:(d + b2 + 1) * 128],
                                 banks[i][:, 0:128]),
                             waits=[p.w("tdn")] +
                             ([p.use("tstage")] if b == 0 else []),
                             inc="vdn")
                        p.done(f"bank{bi}", "vdn")
                    p.done("ternf", "tdn")
                    if store is not None and store[0] == "up":
                        rb = store[1]
                        p.dma("gpsimd", "gld",
                              wupT3[:, :, rb * 128:(rb + 1) * 128],
                              tstage[:, 0:2048].rearrange(
                                  "p (c t) -> p c t", c=16),
                              waits=[p.w("vdn")])
                        p.done("tstage", "gld")
                    elif store is not None:
                        cc = store[1]
                        p.dma("gpsimd", "gld",
                              wdnT3[:, :, cc * 128:(cc + 1) * 128],
                              tstage[:].rearrange("p (b c) -> p b c", b=64),
                              waits=[p.w("vdn")])
                        p.done("tstage", "gld")

                for tp in range(8):
                    for t2 in range(2):
                        ti = tp * 2 + t2
                        p.dma("gpsimd", "gld", xres[t2][:],
                              x_h[ti * 128:(ti + 1) * 128, :],
                              waits=[p.use(f"xres{t2}")])
                    XR = p.w("gld")
                    for fc in range(16):
                        yi = 2 * (fc % 2)
                        yst = wstage[yi]
                        wst = wstage[yi + 1]
                        p.dma("sync", "qld", yst[:],
                              yT_h[fc * 128:(fc + 1) * 128, :],
                              waits=[p.use(f"wst{yi}")])
                        p.dma("sync", "qld", wst[:],
                              wo_h[fc * 128:(fc + 1) * 128, :],
                              waits=[p.use(f"wst{yi + 1}")])
                        ld = p.w("qld")
                        first, last = fc == 0, fc == 15
                        for gi in range(8):
                            t2, ct = gi // 4, gi % 4
                            bname = f"bank{gi}" if gi < 6 else f"fps{gi - 6}"
                            bw = [p.use(bname)] if first else []
                            p.op("tensor",
                                 lambda e, y=yst, w2=wst, tt=t2, c=ct,
                                 g=gi, f=first, l=last, t0=tp:
                                 e.matmul(
                                     fbank(g),
                                     y[:, (t0 * 2 + tt) * 128:
                                       (t0 * 2 + tt + 1) * 128],
                                     w2[:, c * 512:(c + 1) * 512],
                                     start=f, stop=l),
                                 waits=[ld] + bw,
                                 inc="tdn" if gi == 7 else None)
                        p.done(f"wst{yi}", "tdn")
                        p.done(f"wst{yi + 1}", "tdn")
                    OMM = p.w("tdn")
                    for t2 in range(2):
                        ti = tp * 2 + t2
                        for ct in range(4):
                            gi = t2 * 4 + ct
                            bname = f"bank{gi}" if gi < 6 else f"fps{gi - 6}"
                            p.op("vector",
                                 lambda e, g=gi, c=ct, xr=xres[t2], cn=ct:
                                 e.scalar_tensor_tensor(
                                     f1[:, c * 512:(c + 1) * 512],
                                     fbank(g), 1.0,
                                     xr[:, c * 512:(c + 1) * 512],
                                     ALU.mult, ALU.add,
                                     accum_out=sc("sp%d" % cn)),
                                 waits=[OMM, XR], inc="vdn")
                            p.done(bname, "vdn")
                        p.done(f"xres{t2}", "vdn")
                        p.vbub()
                        p.op("vector", lambda e: e.tensor_add(
                            sc("sums"), sc("sp0"), sc("sp1")),
                            waits=[], inc="vdn")
                        p.op("vector", lambda e: e.tensor_add(
                            sc("t1"), sc("sp2"), sc("sp3")),
                            waits=[], inc="vdn")
                        p.vbub()
                        p.op("vector", lambda e: e.tensor_add(
                            sc("sums"), sc("sums"), sc("t1")),
                            waits=[], inc="vdn")
                        p.vbub()
                        p.op("vector", lambda e: e.tensor_scalar_mul(
                            sc("nm"), sc("sums"), -1.0 / D),
                            waits=[], inc="vdn")
                        p.op("vector", lambda e: e.scalar_tensor_tensor(
                            f4[:], f1[:], 1.0, f1[:], ALU.mult, ALU.mult,
                            accum_out=sc("ssq")),
                            waits=[], inc="vdn")
                        p.vbub()
                        p.op("vector", lambda e: e.tensor_scalar_mul(
                            sc("t1"), sc("ssq"), 1.0 / D),
                            waits=[], inc="vdn")
                        p.op("vector", lambda e: e.tensor_mul(
                            sc("m2t"), sc("nm"), sc("nm")),
                            waits=[], inc="vdn")
                        p.vbub()
                        p.op("vector", lambda e: e.tensor_sub(
                            sc("var"), sc("t1"), sc("m2t")),
                            waits=[], inc="vdn")
                        p.vbub()
                        p.vbub(inc="vdn")
                        vd = p.w("vdn")
                        p.op("scalar", lambda e: e.activation(
                            sc("sd"), sc("var"), AF.Sqrt, bias=epsln[:]),
                            waits=[vd], inc="adn")
                        ad = p.w("adn")
                        p.vbub(waits=[ad])
                        p.vbub()
                        p.op("vector", lambda e: e.reciprocal(
                            sc("rstd"), sc("sd")), waits=[], inc="vdn")
                        p.op("vector", lambda e: e.tensor_scalar(
                            f2[:], f1[:], sc("nm"), None, ALU.add),
                            waits=[], inc="vdn")
                        p.op("vector", lambda e: e.tensor_scalar(
                            f3[:], f2[:], sc("rstd"), None, ALU.mult),
                            waits=[p.use("f3st")], inc="vdn")
                        p.dma("gpsimd", "gld",
                              x1_h[ti * 128:(ti + 1) * 128, :], f3[:],
                              waits=[p.w("vdn")])
                        p.done("f3st", "gld")
                        p.op("vector", lambda e: e.scalar_tensor_tensor(
                            f4[:], f3[:], 1.0, f3[:], ALU.mult, ALU.mult,
                            accum_out=sc("ssq")),
                            waits=[], inc="vdn")
                        p.vbub()
                        p.op("vector", lambda e: e.tensor_scalar_mul(
                            sc("t1"), sc("ssq"), 1.0 / D),
                            waits=[], inc="vdn")
                        p.vbub()
                        p.vbub(inc="vdn")
                        vd = p.w("vdn")
                        p.op("scalar", lambda e: e.activation(
                            sc("sd"), sc("t1"), AF.Sqrt, bias=epsrms[:]),
                            waits=[vd], inc="adn")
                        ad = p.w("adn")
                        p.vbub(waits=[ad])
                        p.vbub()
                        p.op("vector", lambda e: e.reciprocal(
                            sc("rstd"), sc("sd")), waits=[], inc="vdn")
                        p.vbub()
                        p.op("vector", lambda e: e.tensor_scalar(
                            f2[:], f3[:], sc("rstd"), None, ALU.mult),
                            waits=[], inc="vdn")
                        p.op("vector", lambda e: e.tensor_mul(
                            f4[:], f2[:], nw_rep[:]),
                            waits=[NWLD], inc="vdn")
                        p.op("vector", lambda e: e.reduce_max(
                            sc("gmax"), f4[:], axis=AX.X,
                            apply_absolute_value=True),
                            waits=[], inc="vdn")
                        p.vbub()
                        p.op("vector", lambda e: e.tensor_scalar_max(
                            sc("gclp"), sc("gmax"), 1e-8),
                            waits=[], inc="vdn")
                        p.vbub()
                        p.op("vector", lambda e: e.reciprocal(
                            sc("rg"), sc("gclp")), waits=[], inc="vdn")
                        p.vbub()
                        p.op("vector", lambda e: e.tensor_scalar_mul(
                            sc("c"), sc("rg"), 127.0), waits=[], inc="vdn")
                        p.op("vector", lambda e: e.tensor_scalar_mul(
                            sc("sx"), sc("gclp"), 1.0 / 127.0),
                            waits=[], inc="vdn")
                        p.vbub()
                        p.op("vector", lambda e, t=ti: e.tensor_scalar(
                            sxs_stash[:, t:t + 1], sc("sx"),
                            sval_bc[0][:], None, ALU.mult),
                            waits=[THRLD], inc="vdn")
                        p.op("vector", lambda e: e.tensor_scalar(
                            f2[:], f4[:], sc("c"), None, ALU.mult),
                            waits=[], inc="vdn")
                        p.op("vector", lambda e: e.tensor_copy(
                            i32t[:], f2[:]), waits=[], inc="vdn")
                        p.op("vector", lambda e: e.tensor_copy(
                            f5[:], i32t[:]), waits=[p.use("f5")],
                            inc="vdn")
                        p.op("vector", lambda e: e.tensor_sub(
                            f1[:], f2[:], f5[:]), waits=[], inc="vdn")
                        p.op("vector", lambda e: e.tensor_single_scalar(
                            f4[:], f1[:], 0.5, ALU.is_ge),
                            waits=[], inc="vdn")
                        p.op("vector", lambda e: e.tensor_add(
                            f5[:], f5[:], f4[:]), waits=[], inc="vdn")
                        p.op("vector", lambda e: e.tensor_single_scalar(
                            f4[:], f1[:], -0.5, ALU.is_lt),
                            waits=[], inc="vdn")
                        p.op("vector", lambda e: e.tensor_sub(
                            f5[:], f5[:], f4[:]), waits=[], inc="vdn")
                        QED = p.w("vdn")
                        for dc in range(16):
                            bk = banks[dc % 2]
                            p.op("tensor",
                                 lambda e, d=dc, b=bk: e.transpose(
                                     b[:, 0:128],
                                     f5[:, d * 128:(d + 1) * 128],
                                     ident[:]),
                                 waits=[QED, p.use(f"bank{dc % 2}")],
                                 inc="tdn")
                            p.op("vector",
                                 lambda e, d=dc, b=bk, t=ti: e.tensor_scalar(
                                     xqT[:, d * 2048 + t * 128:
                                         d * 2048 + (t + 1) * 128],
                                     b[:, 0:128], 127.0, -127.0,
                                     ALU.min, ALU.max),
                                 waits=[p.w("tdn")], inc="vdn")
                            p.done(f"bank{dc % 2}", "vdn")
                        p.done("f5", "tdn")
                    emit_e2_unit(4 * tp)
                    emit_e2_unit(4 * tp + 1)
                    emit_e2_unit(4 * tp + 2)
                    emit_e2_unit(4 * tp + 3)
                for u in range(32, 256):
                    emit_e2_unit(u)
                sxv = p.w("vdn")
                p.op("tensor", lambda e: e.transpose(
                    banks[0][0:16, 0:128], sxs_stash[:], ident[:]),
                    waits=[sxv, p.use("bank0")], inc="tdn")
                p.op("vector", lambda e: e.tensor_copy(
                    sxrow[:], banks[0][0:16, 0:128]),
                    waits=[p.w("tdn")], inc="vdn")
                p.dma("gpsimd", "gld", vecs2[:], sxrow[:],
                      waits=[p.w("vdn")])
                p.fence()

        gctx = ExitStack()
        with gctx:
            gsb = lambda name, shape, dt: gctx.enter_context(
                nc.sbuf_tensor(name, shape, dt))
            sxg = [gsb(f"sxg{i}", [128, 512], F32) for i in range(2)]
            gam_rep = gsb("gam_rep", [128, 2048], F32)
            hfhi = gsb("hfhi", [128, 8 * 512], BF16)
            hflo = gsb("hflo", [128, 8 * 512], BF16)
            oacc = gsb("oacc", [128, 4 * 2048], F32)
            wub = [gsb(f"wub{i}", [128, 16 * 128], BF16) for i in range(2)]
            wdnb = [gsb(f"wdnb{i}", [128, 2048], BF16) for i in range(2)]
            g1 = gsb("g1", [128, 2048], F32)
            g2 = gsb("g2", [128, 2048], F32)
            sfu = gsb("sfu", [128, 512], F32)
            hbuf = gsb("hbuf", [128, 512], F32)
            x1r = [gsb(f"x1r{i}", [128, 2048], F32) for i in range(2)]
            gps = gctx.enter_context(nc.psum_tensor("gps", [128, 1024], F32))

            def gbank(g):
                return (banks[g][:] if g < 6
                        else gps[:, (g - 6) * 512:(g - 5) * 512])

            p.dma("sync", "qld", gam_rep[:],
                  gam_h.partition_broadcast(128), waits=[])
            GAMLD = p.w("qld")

            for tg in range(4):
                p.dma("sync", "qld", sxg[tg % 2][:],
                      vecs_h[:, tg * 512:(tg + 1) * 512]
                      .partition_broadcast(128),
                      waits=[p.use(f"sxg{tg % 2}")])
                SXLD = p.w("qld")
                for e8 in range(8):
                    for s in range(8):
                        hsub = e8 * 8 + s
                        p.dma("sync", "qld", wub[s % 2][:],
                              wupT3[:, :, hsub * 128:(hsub + 1) * 128],
                              waits=[p.use(f"wub{s % 2}")])
                        wl = p.w("qld")
                        ubn = f"bank{4 + s % 2}"
                        upb = banks[4 + s % 2]
                        for dc in range(16):
                            bw = [p.use(ubn)] if dc == 0 else []
                            p.op("tensor",
                                 lambda e, w2=wub[s % 2], d=dc, b=upb,
                                 f=(dc == 0), l=(dc == 15), t0=tg:
                                 e.matmul(
                                     b[:], w2[:, d * 128:(d + 1) * 128],
                                     xqT[:, d * 2048 + t0 * 512:
                                         d * 2048 + (t0 + 1) * 512],
                                     start=f, stop=l),
                                 waits=[wl] + bw,
                                 inc="tdn" if dc == 15 else None)
                        p.done(f"wub{s % 2}", "tdn")
                        um = p.w("tdn")
                        p.op("vector",
                             lambda e, b=upb, t0=tg: e.tensor_mul(
                                 sfu[:], b[:], sxg[t0 % 2][:]),
                             waits=[um, SXLD], inc="vdn")
                        p.done(ubn, "vdn")
                        vd = p.w("vdn")
                        p.op("scalar",
                             lambda e: e.activation(hbuf[:], sfu[:], AF.Silu),
                             waits=[vd], inc="adn")
                        ad = p.w("adn")
                        p.op("vector",
                             lambda e, s2=s: e.tensor_copy(
                                 hfhi[:, s2 * 512:(s2 + 1) * 512], hbuf[:]),
                             waits=[ad, p.use("hfeat")], inc="vdn")
                        p.op("vector",
                             lambda e, s2=s: e.scalar_tensor_tensor(
                                 hflo[:, s2 * 512:(s2 + 1) * 512], hbuf[:],
                                 1.0, hfhi[:, s2 * 512:(s2 + 1) * 512],
                                 ALU.mult, ALU.subtract),
                             waits=[], inc="vdn")
                    HFD = p.w("vdn")
                    for half in range(2):
                        for s in range(8):
                            hsub = e8 * 8 + s
                            p.dma("sync", "qld", wdnb[s % 2][:],
                                  wdnT_h[hsub * 128:(hsub + 1) * 128, :],
                                  waits=[p.use(f"wdnb{s % 2}")])
                            wl = p.w("qld")
                            for t2 in range(2):
                                tt = half * 2 + t2
                                for ct in range(4):
                                    gi = t2 * 4 + ct
                                    bname = (f"bank{gi}" if gi < 6
                                             else f"gps{gi - 6}")
                                    for hilo in range(2):
                                        hf = hfhi if hilo == 0 else hflo
                                        st = s == 0 and hilo == 0
                                        sp = s == 7 and hilo == 1
                                        bw = [p.use(bname)] if st else []
                                        p.op("tensor",
                                             lambda e, h3=hf, s2=s, t3=tt,
                                             c=ct, g=gi, w2=wdnb[s % 2],
                                             f=st, l=sp: e.matmul(
                                                 gbank(g),
                                                 h3[:, s2 * 512 + t3 * 128:
                                                    s2 * 512 + (t3 + 1) * 128],
                                                 w2[:, c * 512:(c + 1) * 512],
                                                 start=f, stop=l),
                                             waits=[HFD, wl] + bw,
                                             inc="tdn"
                                             if (hilo == 1 and ct == 3 and
                                                 t2 == 1) else None)
                            p.done(f"wdnb{s % 2}", "tdn")
                        dmm = p.w("tdn")
                        for t2 in range(2):
                            tt = half * 2 + t2
                            for ct in range(4):
                                gi = t2 * 4 + ct
                                bname = (f"bank{gi}" if gi < 6
                                         else f"gps{gi - 6}")
                                col = tt * 2048 + ct * 512
                                if e8 == 0:
                                    p.op("vector",
                                         lambda e, g=gi, c2=col:
                                         e.tensor_copy(
                                             oacc[:, c2:c2 + 512], gbank(g)),
                                         waits=[dmm], inc="vdn")
                                else:
                                    p.op("vector",
                                         lambda e, g=gi, c2=col:
                                         e.scalar_tensor_tensor(
                                             oacc[:, c2:c2 + 512], gbank(g),
                                             1.0, oacc[:, c2:c2 + 512],
                                             ALU.mult, ALU.add),
                                         waits=[dmm], inc="vdn")
                                p.done(bname, "vdn")
                    p.done("hfeat", "tdn")
                p.done(f"sxg{tg % 2}", "vdn")
                for tt in range(4):
                    ti = tg * 4 + tt
                    p.dma("gpsimd", "gld", x1r[tt % 2][:],
                          x1_h[ti * 128:(ti + 1) * 128, :],
                          waits=[p.use(f"x1r{tt % 2}")])
                    X1L = p.w("gld")
                    p.op("vector",
                         lambda e, t3=tt: e.tensor_mul(
                             g1[:], oacc[:, t3 * 2048:(t3 + 1) * 2048],
                             gam_rep[:]),
                         waits=[GAMLD, p.use("g1")], inc="vdn")
                    p.op("vector", lambda e: e.tensor_scalar(
                        g2[:], g1[:], sval_bc[1][:], None, ALU.mult),
                        waits=[], inc="vdn")
                    p.op("vector",
                         lambda e, t3=tt: e.scalar_tensor_tensor(
                             g1[:], g2[:], 1.0, x1r[t3 % 2][:],
                             ALU.mult, ALU.add, accum_out=sc("sums")),
                         waits=[X1L], inc="vdn")
                    p.done(f"x1r{tt % 2}", "vdn")
                    p.op("vector", lambda e: e.scalar_tensor_tensor(
                        g2[:], g1[:], 1.0, g1[:], ALU.mult, ALU.mult,
                        accum_out=sc("ssq")),
                        waits=[], inc="vdn")
                    p.vbub()
                    p.op("vector", lambda e: e.tensor_scalar_mul(
                        sc("nm"), sc("sums"), -1.0 / D),
                        waits=[], inc="vdn")
                    p.op("vector", lambda e: e.tensor_scalar_mul(
                        sc("t1"), sc("ssq"), 1.0 / D),
                        waits=[], inc="vdn")
                    p.vbub()
                    p.op("vector", lambda e: e.tensor_mul(
                        sc("m2t"), sc("nm"), sc("nm")),
                        waits=[], inc="vdn")
                    p.vbub()
                    p.op("vector", lambda e: e.tensor_sub(
                        sc("var"), sc("t1"), sc("m2t")),
                        waits=[], inc="vdn")
                    p.vbub()
                    p.vbub(inc="vdn")
                    vd = p.w("vdn")
                    p.op("scalar", lambda e: e.activation(
                        sc("sd"), sc("var"), AF.Sqrt, bias=epsln[:]),
                        waits=[vd], inc="adn")
                    ad = p.w("adn")
                    p.vbub(waits=[ad])
                    p.vbub()
                    p.op("vector", lambda e: e.reciprocal(
                        sc("rstd"), sc("sd")), waits=[], inc="vdn")
                    p.vbub()
                    p.op("vector", lambda e: e.tensor_scalar(
                        g2[:], g1[:], sc("nm"), None, ALU.add),
                        waits=[], inc="vdn")
                    p.op("vector", lambda e: e.tensor_scalar(
                        g1[:], g2[:], sc("rstd"), None, ALU.mult),
                        waits=[], inc="vdn")
                    p.dma("gpsimd", "gld",
                          out_h[ti * 128:(ti + 1) * 128, :], g1[:],
                          waits=[p.w("vdn")])
                    p.done("g1", "gld")
            p.fence()

        with nc.Block() as block:
            p.emit(block, sems)

    return nc


_NC_CACHE = [None]


def kernel(L0_attn_x, L0_attn_Wq, L0_attn_Wk, L0_attn_Wv, L0_attn_Wo,
           L0_ffn_weight_up, L0_ffn_weight_down, L0_ffn_norm_weight,
           L0_ffn_gamma):
    x = np.ascontiguousarray(np.asarray(L0_attn_x, dtype=np.float32))
    wq = np.ascontiguousarray(np.asarray(L0_attn_Wq, dtype=np.float32))
    wk = np.ascontiguousarray(np.asarray(L0_attn_Wk, dtype=np.float32))
    wv = np.ascontiguousarray(np.asarray(L0_attn_Wv, dtype=np.float32))
    wo = np.ascontiguousarray(np.asarray(L0_attn_Wo, dtype=np.float32))
    wup = np.ascontiguousarray(np.asarray(L0_ffn_weight_up, dtype=np.float32))
    wdn = np.ascontiguousarray(np.asarray(L0_ffn_weight_down, dtype=np.float32))
    nw = np.ascontiguousarray(
        np.asarray(L0_ffn_norm_weight, dtype=np.float32).reshape(1, D))
    gam = np.ascontiguousarray(
        np.asarray(L0_ffn_gamma, dtype=np.float32).reshape(1, D))

    if _NC_CACHE[0] is None:
        _NC_CACHE[0] = build()
    nc = _NC_CACHE[0]

    in_maps = []
    for c in range(8):
        in_maps.append({
            "x": np.ascontiguousarray(x[c]),
            "wq": wq, "wk": wk, "wv": wv, "wo": wo,
            "wup": wup, "wdn": wdn, "nw": nw, "gam": gam,
        })
    res = run_bass_kernel_spmd(nc, in_maps, core_ids=list(range(8)))
    return np.stack([r["out"] for r in res.results], axis=0)


if __name__ == "__main__":
    nc = build()
    print("build ok")
